# version 01 (final)
# baseline (speedup 1.0000x reference)
import contextlib
import numpy as np
import ml_dtypes
import concourse.bass as bass
import concourse.mybir as mybir
from concourse.bass_utils import run_bass_kernel_spmd

F32 = mybir.dt.float32
BF16 = mybir.dt.bfloat16
AF = mybir.ActivationFunctionType
ALU = mybir.AluOpType
AX = mybir.AxisListType

TOK = 2048
D = 1024
HID = 2048
DI = 3072
F1 = 6144
CAP = 5120.0
SQH = float(np.sqrt(2048.0))
ENG_ATTR = {"te": "tensor", "act": "scalar", "dve": "vector", "pool": "gpsimd", "sp": "sync"}


class Prog:
    ENGS = ("te", "act", "dve", "pool", "sp")

    def __init__(self, nc, block, sems):
        self.nc = nc
        self.block = block
        self.sems = sems
        self.buf = {e: [] for e in self.ENGS}
        self.cnt = dict.fromkeys(self.ENGS, 0)
        self.lastw = {}

    def op(self, eng, fn, inc=1):
        self.cnt[eng] += inc
        self.buf[eng].append((fn, inc, None))

    def dma(self, q, out_ap, in_ap, **kw):
        self.op(q, lambda e, o=out_ap, i=in_ap, k=dict(kw): e.dma_start(o, i, **k), inc=16)

    def wait(self, eng, *prods):
        if prods != (eng,):
            prods = tuple(p for p in self.ENGS if p != eng)
        for p in prods:
            v = self.cnt[p]
            if v and self.lastw.get((eng, p), 0) < v:
                self.lastw[(eng, p)] = v
                self.buf[eng].append((None, 0, (p, v)))

    def selfsync(self, eng):
        self.wait(eng, eng)

    def flush(self):
        for eng in self.ENGS:
            items = self.buf[eng]
            if not items:
                continue
            self.buf[eng] = []
            sems = self.sems

            def mk(items=items, eng=eng):
                def run(e):
                    for fn, inc, w in items:
                        if w is not None:
                            e.wait_ge(sems[w[0]], w[1])
                        else:
                            fn(e).then_inc(sems[eng], inc)
                return run
            getattr(self.block, ENG_ATTR[eng])(mk())

    def finish(self):
        for e in self.ENGS:
            self.wait(e, "sp", "pool")
        self.flush()


class Builder:

    def __init__(self, name):
        self.nc = bass.Bass("TRN2", target_bir_lowering=False)
        self.stack = contextlib.ExitStack()

    def start(self):
        nc = self.nc
        sems = {}
        for n in Prog.ENGS:
            sems[n] = self.stack.enter_context(nc.semaphore("sem_" + n))
        block = self.stack.enter_context(nc.Block())
        self.P = Prog(nc, block, sems)
        return self.P

    def sb(self, name, shape, dt):
        return self.stack.enter_context(self.nc.sbuf_tensor(name, shape, dt))

    def scoped(self):
        return self.stack.enter_context(contextlib.ExitStack())

    def close(self):
        self.P.finish()
        self.stack.close()
        return self.nc


def consts(B, P):
    nc = B.nc
    B.onesf = nc.alloc_sbuf_tensor("onesf", [128, 128], F32)
    B.onesb = nc.alloc_sbuf_tensor("onesb", [128, 128], BF16)
    B.idf = nc.alloc_sbuf_tensor("idf", [128, 128], F32)
    P.op("dve", lambda e: e.memset(B.onesf[:, :], 1.0))
    P.op("dve", lambda e: e.memset(B.onesb[:, :], 1.0))
    P.wait("pool", "dve")
    B.idb = nc.alloc_sbuf_tensor("idb", [128, 128], BF16)
    B.ceps = nc.alloc_sbuf_tensor("ceps", [128, 1], F32)
    P.op("dve", lambda e: e.memset(B.ceps[:, :], 1e-5))
    P.op("pool", lambda e: e.affine_select(B.idf[:, :], B.onesf[:, :], [[1, 128]],
                                           ALU.is_equal, 0.0, base=0, channel_multiplier=-1))
    P.op("pool", lambda e: e.affine_select(B.idb[:, :], B.onesb[:, :], [[1, 128]],
                                           ALU.is_equal, 0.0, base=0, channel_multiplier=-1))


def tr_f32(B, P, src_ap, dst_bf_ap, scale_col=None, paux=None):
    P.wait("te", "act", "dve")
    P.op("te", lambda e: e.transpose(paux[:, 0:128], src_ap, B.idf[:, :]))
    P.wait("dve", "te")
    if scale_col is None:
        P.op("dve", lambda e: e.tensor_copy(dst_bf_ap, paux[:, 0:128]))
    else:
        P.op("dve", lambda e: e.tensor_scalar(dst_bf_ap, paux[:, 0:128], scale_col, None, ALU.mult))


def ln_tile(B, P, src_ap, xs, mcol, paux, xnt_dst_fn):
    P.wait("dve", "sp", "act", "te")
    P.op("dve", lambda e: e.reduce_sum(mcol[:, 0:1], src_ap, axis=AX.X))
    P.selfsync("dve")
    P.op("dve", lambda e: e.tensor_scalar(mcol[:, 1:2], mcol[:, 0:1], -1.0 / 1024.0, None, ALU.mult))
    P.selfsync("dve")
    P.op("dve", lambda e: e.tensor_scalar(xs[:, 0:1024], src_ap, mcol[:, 1:2], None, ALU.add))
    P.selfsync("dve")
    P.op("dve", lambda e: e.tensor_mul(xs[:, 1024:2048], xs[:, 0:1024], xs[:, 0:1024]))
    P.selfsync("dve")
    P.op("dve", lambda e: e.reduce_sum(mcol[:, 0:1], xs[:, 1024:2048], axis=AX.X))
    P.wait("act", "dve")
    P.op("act", lambda e: e.activation(mcol[:, 2:3], mcol[:, 0:1], AF.Ln, bias=B.ceps[:, 0:1], scale=1.0 / 1024.0))
    P.selfsync("act")
    P.op("act", lambda e: e.activation(mcol[:, 3:4], mcol[:, 2:3], AF.Exp, scale=-0.5))
    P.wait("dve", "act")
    P.op("dve", lambda e: e.tensor_scalar(xs[:, 0:1024], xs[:, 0:1024], mcol[:, 3:4], None, ALU.mult))
    P.selfsync("dve")
    for dt in range(8):
        tr_f32(B, P, xs[:, dt * 128:dt * 128 + 128], xnt_dst_fn(dt), paux=paux)



def dma_kload(P, q, dst2d, src2d, k, nsplit=1):
    src3 = src2d.rearrange("(k p) f -> p k f", p=128)
    dst3 = dst2d.rearrange("p (k f) -> p k f", k=k)
    step = k // nsplit
    for s0 in range(0, k, step):
        P.dma(q, dst3[:, s0:s0 + step], src3[:, s0:s0 + step])


def build_A(stage=99):
    B = Builder("A")
    nc = B.nc
    x_d = nc.dram_tensor("x", [TOK, D], F32, kind="ExternalInput")
    Wq_d = nc.dram_tensor("Wq", [D, 16, 64], F32, kind="ExternalInput")
    bq_d = nc.dram_tensor("bq", [16, 64], F32, kind="ExternalInput")
    Wk_d = nc.dram_tensor("Wk", [D, 16, 64], F32, kind="ExternalInput")
    bk_d = nc.dram_tensor("bk", [16, 64], F32, kind="ExternalInput")
    Wv_d = nc.dram_tensor("Wv", [D, 16, 64], F32, kind="ExternalInput")
    bv_d = nc.dram_tensor("bv", [16, 64], F32, kind="ExternalInput")
    Wo_d = nc.dram_tensor("Wo", [16, 64, D], F32, kind="ExternalInput")
    bo_d = nc.dram_tensor("bo", [D], F32, kind="ExternalInput")
    rm_d = nc.dram_tensor("rm", [4, 64, 16, 16], F32, kind="ExternalInput")
    rup_d = nc.dram_tensor("rup", [4, 64, 64, 16], F32, kind="ExternalInput")
    wc_d = nc.dram_tensor("wc", [D, 16384], F32, kind="ExternalInput")
    Wp_d = nc.dram_tensor("Wp", [D, HID], F32, kind="ExternalInput")
    bp_d = nc.dram_tensor("bp", [HID], F32, kind="ExternalInput")
    g1_d = nc.dram_tensor("g1", [4, HID], F32, kind="ExternalInput")
    Whg_d = nc.dram_tensor("Whg", [4, HID, F1], F32, kind="ExternalInput")
    bhg_d = nc.dram_tensor("bhg", [4, F1], F32, kind="ExternalInput")
    Wo1_d = nc.dram_tensor("Wo1", [4, DI, HID], F32, kind="ExternalInput")
    bo1_d = nc.dram_tensor("bo1", [4, HID], F32, kind="ExternalInput")
    g2_d = nc.dram_tensor("g2", [4, HID], F32, kind="ExternalInput")
    gf_d = nc.dram_tensor("gf", [HID], F32, kind="ExternalInput")
    Wf_d = nc.dram_tensor("Wf", [HID, D], F32, kind="ExternalInput")
    bf_d = nc.dram_tensor("bf", [D], F32, kind="ExternalInput")
    Wr_d = nc.dram_tensor("Wr", [8, D], F32, kind="ExternalInput")

    part_d = nc.dram_tensor("part", [TOK, D], F32, kind="ExternalOutput")
    xnt1_d = nc.dram_tensor("xnt1", [D, TOK], BF16, kind="ExternalOutput")
    logits_d = nc.dram_tensor("logits", [TOK, 8], F32, kind="ExternalOutput")
    x1_d = nc.dram_tensor("x1scr", [TOK, D], F32, kind="Internal")
    h_d = nc.dram_tensor("hscr", [TOK, HID], BF16, kind="Internal")

    pb = [nc.alloc_psum_tensor(f"pb{i}", [128, 512], F32) for i in range(6)]
    ptb = nc.alloc_psum_tensor("ptb", [128, 1024], BF16)
    paux = nc.alloc_psum_tensor("paux", [128, 512], F32)

    P = B.start()
    consts(B, P)

    g_res = nc.sbuf_tensor("res", [128, 16 * 1024], F32, side="right")
    res = g_res.__enter__()
    g_xnt = nc.sbuf_tensor("xnt", [128, 8 * 2048], BF16, side="right")
    xnt = g_xnt.__enter__()

    with nc.sbuf_tensor("ln_s", [128, 2048], F32) as xs, \
         nc.sbuf_tensor("ln_m", [128, 4], F32) as mcol:
        for m in range(16):
            P.wait("sp", "dve", "te", "act")
            P.dma("sp", res[:, m * 1024:(m + 1) * 1024], x_d[m * 128:(m + 1) * 128, :])
            P.wait("dve", "sp")
            ln_tile(B, P, res[:, m * 1024:(m + 1) * 1024], xs, mcol, paux,
                    lambda dt, m=m: xnt[:, dt * 2048 + m * 128: dt * 2048 + (m + 1) * 128])
        P.flush()
    if stage == 1:
        P.wait("pool", "sp")
        for m in range(16):
            P.dma("pool", part_d[m * 128:(m + 1) * 128, :], res[:, m * 1024:(m + 1) * 1024])
        for dt in range(8):
            P.dma("pool", xnt1_d[dt * 128:(dt + 1) * 128, :], xnt[:, dt * 2048:(dt + 1) * 2048])
        return B.close()

    g_ctxn = nc.sbuf_tensor("ctxn", [128, 8 * 2048], BF16)
    ctxn = g_ctxn.__enter__()
    with nc.sbuf_tensor("qT", [128, 4 * 2048], BF16) as qT, \
         nc.sbuf_tensor("kT", [128, 4 * 2048], BF16) as kT, \
         nc.sbuf_tensor("vsb", [128, 8 * 16 * 65], BF16) as vsb, \
         nc.sbuf_tensor("wstg", [128, 1024], F32) as wstg, \
         nc.sbuf_tensor("wqkv", [128, 8 * 128], BF16) as wqkv, \
         nc.sbuf_tensor("bcol", [128, 4], F32) as bcol, \
         nc.sbuf_tensor("pTs", [128, 512], BF16) as pTs, \
         nc.sbuf_tensor("rcs", [128, 2], F32) as rcs, \
         nc.sbuf_tensor("dens", [128, 512], F32) as dens:
        for grp in range(2):
            for hpl in range(4):
                hp = 4 * grp + hpl
                for which, W_d, b_d, dst, scale in (
                        (0, Wq_d, bq_d, qT, 0.125), (1, Wk_d, bk_d, kT, 1.0)):
                    P.wait("sp", "te", "dve", "act")
                    src = W_d[:, 2 * hp:2 * hp + 2, :].rearrange("(k p) a b -> p k (a b)", p=128)
                    P.dma("sp", wstg[:, :].rearrange("p (k f) -> p k f", k=8), src)
                    P.dma("sp", bcol[:, 0:1], b_d.ap().rearrange("h (k o) -> (h k) o", o=1)[128 * hp:128 * hp + 128, :])
                    P.wait("dve", "sp")
                    P.op("dve", lambda e: e.tensor_copy(wqkv[:, :], wstg[:, :]))
                    P.op("dve", lambda e, s=scale: e.tensor_scalar(bcol[:, 1:2], bcol[:, 0:1], s, None, ALU.mult))
                    P.selfsync("dve")
                    for tc in range(4):
                        P.wait("te", "act", "dve")
                        for dt in range(8):
                            P.op("te", lambda e, dt=dt, tc=tc: e.matmul(
                                pb[0][:, :], wqkv[:, dt * 128:(dt + 1) * 128],
                                xnt[:, dt * 2048 + tc * 512: dt * 2048 + (tc + 1) * 512],
                                start=(dt == 0), stop=(dt == 7)))
                        P.wait("act", "te")
                        P.op("act", lambda e, d=dst, hpl=hpl, tc=tc, s=scale: e.activation(
                            d[:, hpl * 2048 + tc * 512: hpl * 2048 + (tc + 1) * 512],
                            pb[0][:, :], AF.Identity, bias=bcol[:, 1:2], scale=s))
            if stage == 21:
                continue
            P.wait("dve", "te", "act")
            P.op("dve", lambda e: e.memset(vsb[:, :], 1.0))
            for hpl in range(4):
                hp = 4 * grp + hpl
                P.wait("sp", "te", "dve", "act")
                src = Wv_d[:, 2 * hp:2 * hp + 2, :].rearrange("(k p) a b -> p k (a b)", p=128)
                P.dma("sp", wstg[:, :].rearrange("p (k f) -> p k f", k=8), src)
                P.wait("dve", "sp")
                P.op("dve", lambda e: e.tensor_copy(wqkv[:, :], wstg[:, :]))
                P.selfsync("dve")
                for st in range(16):
                    P.wait("te", "act", "dve")
                    for dt in range(8):
                        P.op("te", lambda e, dt=dt, st=st: e.matmul(
                            pb[1][:, 0:128], xnt[:, dt * 2048 + st * 128: dt * 2048 + (st + 1) * 128],
                            wqkv[:, dt * 128:(dt + 1) * 128], start=(dt == 0), stop=(dt == 7)))
                    P.wait("dve", "te")
                    for par in range(2):
                        hl = 2 * hpl + par
                        P.op("dve", lambda e, hl=hl, st=st, par=par: e.tensor_copy(
                            vsb[:, (hl * 16 + st) * 65:(hl * 16 + st) * 65 + 64],
                            pb[1][:, par * 64:par * 64 + 64]))
            if stage == 22:
                continue
            for hl in range(8):
                hpl, par = hl // 2, hl % 2
                for qc in range(4):
                    for st in range(16):
                        bank = pb[2 + (st % 2)]
                        P.wait("te", "act", "dve")
                        P.op("te", lambda e, b=bank, hpl=hpl, par=par, st=st, qc=qc: e.matmul(
                            b[:, :], kT[64 * par:64 * par + 64, hpl * 2048 + st * 128: hpl * 2048 + (st + 1) * 128],
                            qT[64 * par:64 * par + 64, hpl * 2048 + qc * 512: hpl * 2048 + (qc + 1) * 512],
                            start=True, stop=True))
                        P.wait("act", "te")
                        P.op("act", lambda e, b=bank: e.activation(pTs[:, :], b[:, :], AF.Exp))
                        P.wait("te", "act")
                        P.op("te", lambda e, hl=hl, st=st: e.matmul(
                            pb[4][0:65, :], vsb[:, (hl * 16 + st) * 65:(hl * 16 + st) * 65 + 65],
                            pTs[:, :], start=(st == 0), stop=(st == 15)))
                    P.wait("dve", "te")
                    P.op("dve", lambda e: e.reciprocal(dens[0:1, :], pb[4][64:65, :]))
                    P.selfsync("dve")
                    P.wait("te", "dve")
                    P.op("te", lambda e: e.matmul(pb[5][0:64, :], B.onesf[0:1, 0:64], dens[0:1, :],
                                                  start=True, stop=True))
                    P.wait("dve", "te")
                    P.op("dve", lambda e: e.tensor_copy(dens[0:64, :], pb[5][0:64, :]))
                    P.selfsync("dve")
                    hg = 4 * grp + hpl
                    P.op("dve", lambda e, hg=hg, par=par, qc=qc: e.tensor_mul(
                        ctxn[64 * par:64 * par + 64, hg * 2048 + qc * 512: hg * 2048 + (qc + 1) * 512],
                        pb[4][0:64, :], dens[0:64, :]))
        P.flush()
    if stage in (21, 22, 23):
        P.wait("pool", "sp")
        for m in range(16):
            P.dma("pool", part_d[m * 128:(m + 1) * 128, :], res[:, m * 1024:(m + 1) * 1024])
        return B.close()

    with nc.sbuf_tensor("wob", [128, 8 * 1024], BF16) as wob, \
         nc.sbuf_tensor("wstg2", [128, 1024], F32) as wstg2, \
         nc.sbuf_tensor("bvc", [128, 17], BF16) as bvc, \
         nc.sbuf_tensor("bvf", [128, 17], F32) as bvf, \
         nc.sbuf_tensor("brow", [1, 2048], F32) as brow:
        for i in range(8):
            P.wait("sp", "dve")
            P.dma("sp", wstg2[:, :], Wo_d[2 * i:2 * i + 2].rearrange("a k d -> (a k) d"))
            P.wait("dve", "sp")
            P.op("dve", lambda e, i=i: e.tensor_copy(wob[:, i * 1024:(i + 1) * 1024], wstg2[:, :]))
        for h in range(16):
            P.dma("sp", bvf[64 * (h % 2):64 * (h % 2) + 64, h:h + 1],
                  bv_d.ap().rearrange("h (k o) -> (h k) o", o=1)[64 * h:64 * h + 64, :])
        P.dma("sp", brow[0:1, 0:1024], bo_d.ap().rearrange("(o d) -> o d", o=1))
        P.wait("dve", "sp")
        P.op("dve", lambda e: e.tensor_copy(bvc[:, :], bvf[:, :]))
        P.selfsync("dve")
        for dc in range(2):
            P.wait("te", "dve", "act")
            for h in range(16):
                bank = paux if h % 2 == 0 else pb[2]
                P.op("te", lambda e, h=h, dc=dc, bank=bank: e.matmul(
                    bank[0:1, :], bvc[64 * (h % 2):64 * (h % 2) + 64, h:h + 1],
                    wob[64 * (h % 2):64 * (h % 2) + 64, (h // 2) * 1024 + dc * 512:(h // 2) * 1024 + (dc + 1) * 512],
                    start=(h < 2), stop=(h >= 14)))
            P.wait("dve", "te")
            P.op("dve", lambda e, dc=dc: e.tensor_copy(brow[0:1, 1024 + dc * 512:1024 + (dc + 1) * 512],
                                                       paux[0:1, :]))
            P.selfsync("dve")
            P.op("dve", lambda e, dc=dc: e.tensor_add(brow[0:1, 1024 + dc * 512:1024 + (dc + 1) * 512],
                                                      brow[0:1, 1024 + dc * 512:1024 + (dc + 1) * 512],
                                                      pb[2][0:1, :]))
            P.selfsync("dve")
            P.op("dve", lambda e, dc=dc: e.tensor_add(brow[0:1, 1024 + dc * 512:1024 + (dc + 1) * 512],
                                                      brow[0:1, 1024 + dc * 512:1024 + (dc + 1) * 512],
                                                      brow[0:1, dc * 512:(dc + 1) * 512]))
        P.selfsync("dve")
        with nc.sbuf_tensor("biasbc", [128, 1024], F32) as biasbc:
            for dc in range(2):
                P.wait("te", "dve", "act")
                P.op("te", lambda e, dc=dc: e.matmul(
                    paux[:, :], B.onesf[0:1, 0:128], brow[0:1, 1024 + dc * 512:1024 + (dc + 1) * 512],
                    start=True, stop=True))
                P.wait("dve", "te")
                P.op("dve", lambda e, dc=dc: e.tensor_copy(biasbc[:, dc * 512:(dc + 1) * 512], paux[:, :]))
            P.selfsync("dve")
            for m in range(16):
                for dc in range(2):
                    P.wait("te", "dve", "act")
                    for h in range(16):
                        bank = pb[h % 2]
                        P.op("te", lambda e, h=h, m=m, dc=dc, bank=bank: e.matmul(
                            bank[:, :],
                            ctxn[64 * (h % 2):64 * (h % 2) + 64, (h // 2) * 2048 + m * 128:(h // 2) * 2048 + (m + 1) * 128],
                            wob[64 * (h % 2):64 * (h % 2) + 64, (h // 2) * 1024 + dc * 512:(h // 2) * 1024 + (dc + 1) * 512],
                            start=(h < 2), stop=(h >= 14)))
                    P.wait("dve", "te")
                    P.op("dve", lambda e, m=m, dc=dc: e.tensor_add(
                        res[:, m * 1024 + dc * 512: m * 1024 + (dc + 1) * 512],
                        res[:, m * 1024 + dc * 512: m * 1024 + (dc + 1) * 512], pb[0][:, :]))
                    P.selfsync("dve")
                    P.op("dve", lambda e, m=m, dc=dc: e.tensor_add(
                        res[:, m * 1024 + dc * 512: m * 1024 + (dc + 1) * 512],
                        res[:, m * 1024 + dc * 512: m * 1024 + (dc + 1) * 512], pb[1][:, :]))
                    P.selfsync("dve")
                    P.op("dve", lambda e, m=m, dc=dc: e.tensor_add(
                        res[:, m * 1024 + dc * 512: m * 1024 + (dc + 1) * 512],
                        res[:, m * 1024 + dc * 512: m * 1024 + (dc + 1) * 512], biasbc[:, dc * 512:(dc + 1) * 512]))
            P.flush()
    g_ctxn.__exit__(None, None, None)
    if stage in (2, 24, 25):
        P.wait("pool", "sp")
        for m in range(16):
            P.dma("pool", part_d[m * 128:(m + 1) * 128, :], res[:, m * 1024:(m + 1) * 1024])
        return B.close()

    with nc.sbuf_tensor("wblk1", [128, 8 * 128], BF16) as wblk1, \
         nc.sbuf_tensor("wblk2", [128, 8 * 128], BF16) as wblk2, \
         nc.sbuf_tensor("w1stg", [128, 128], F32) as w1stg, \
         nc.sbuf_tensor("w2stg", [128, 128], F32) as w2stg, \
         nc.sbuf_tensor("rnat", [128, 16], F32) as rnat, \
         nc.sbuf_tensor("tstg", [128, 128], F32) as tstg, \
         nc.sbuf_tensor("yt", [128, 2 * 2048], BF16) as yt, \
         nc.sbuf_tensor("hT", [128, 8 * 2048], BF16) as hT, \
         nc.sbuf_tensor("wcT", [128, 8 * 1024], BF16) as wcT, \
         nc.sbuf_tensor("wcn", [128, 128], F32) as wcn:
        for h in range(4):
            for dt in range(8):
                P.wait("dve", "te", "sp", "pool")
                P.op("dve", lambda e: e.memset(w1stg[:, :], 0.0))
                P.wait("sp", "dve")
                for rp in range(8):
                    P.dma("sp", w1stg[16 * rp:16 * rp + 16, 16 * rp:16 * rp + 16],
                          rm_d[h, 8 * dt + rp])
                P.wait("dve", "sp")
                P.op("dve", lambda e, dt=dt: e.tensor_copy(wblk1[:, dt * 128:(dt + 1) * 128], w1stg[:, :]))
            P.selfsync("dve")
            for c4 in range(4):
                c = h * 4 + c4
                dts = (2 * c4, 2 * c4 + 1)
                for dtl in range(2):
                    dt = dts[dtl]
                    for tc in range(4):
                        P.wait("te", "act", "dve")
                        P.op("te", lambda e, dt=dt, tc=tc: e.matmul(
                            pb[0][:, :], wblk1[:, dt * 128:(dt + 1) * 128],
                            xnt[:, dt * 2048 + tc * 512: dt * 2048 + (tc + 1) * 512],
                            start=True, stop=True))
                        P.wait("dve", "te")
                        P.op("dve", lambda e, dtl=dtl, tc=tc: e.tensor_copy(
                            yt[:, dtl * 2048 + tc * 512: dtl * 2048 + (tc + 1) * 512], pb[0][:, :]))
                for ffl in range(8):
                    ftg = 8 * c + ffl
                    lr = ftg - 32 * h
                    r0 = 2 * lr
                    rbase = 16 * (r0 % 8)
                    dtl = (lr // 4) - 2 * c4
                    P.wait("dve", "te", "sp", "pool")
                    P.op("dve", lambda e: e.memset(w2stg[:, :], 0.0))
                    P.wait("sp", "dve", "pool")
                    P.dma("sp", rnat[:, :], rup_d[h, r0:r0 + 2].rearrange("a k j -> (a k) j"))
                    P.wait("te", "sp")
                    P.op("te", lambda e: e.transpose(paux[0:16, 0:128], rnat[:, :], B.idf[:, :]))
                    P.wait("dve", "te", "sp")
                    P.op("dve", lambda e: e.tensor_copy(tstg[0:16, :], paux[0:16, 0:128]))
                    P.wait("pool", "dve")
                    P.dma("pool", w2stg[rbase:rbase + 16, 0:64], tstg[0:16, 0:64])
                    P.dma("pool", w2stg[rbase + 16:rbase + 32, 64:128], tstg[0:16, 64:128])
                    P.wait("dve", "pool")
                    P.op("dve", lambda e, ffl=ffl: e.tensor_copy(wblk2[:, ffl * 128:(ffl + 1) * 128], w2stg[:, :]))
                    P.selfsync("dve")
                    for tc in range(4):
                        P.wait("te", "act", "dve")
                        P.op("te", lambda e, ffl=ffl, dtl=dtl, tc=tc: e.matmul(
                            pb[1][:, :], wblk2[:, ffl * 128:(ffl + 1) * 128],
                            yt[:, dtl * 2048 + tc * 512: dtl * 2048 + (tc + 1) * 512],
                            start=True, stop=True))
                        P.wait("act", "te")
                        P.op("act", lambda e, ffl=ffl, tc=tc: e.activation(
                            hT[:, ffl * 2048 + tc * 512: ffl * 2048 + (tc + 1) * 512],
                            pb[1][:, :], AF.Gelu))
                    for dt2 in range(8):
                        P.wait("sp", "te", "dve", "act")
                        P.dma("sp", wcn[:, :], wc_d[dt2 * 128:(dt2 + 1) * 128, ftg * 128:(ftg + 1) * 128])
                        P.wait("te", "sp")
                        P.op("te", lambda e: e.transpose(paux[:, 0:128], wcn[:, :], B.idf[:, :]))
                        P.wait("dve", "te")
                        P.op("dve", lambda e, ffl=ffl, dt2=dt2: e.tensor_copy(
                            wcT[:, ffl * 1024 + dt2 * 128: ffl * 1024 + (dt2 + 1) * 128], paux[:, 0:128]))
                P.wait("te", "act", "dve")
                for m in range(16):
                    for dc in range(2):
                        P.wait("te", "dve")
                        for ffl in range(8):
                            P.op("te", lambda e, ffl=ffl, m=m, dc=dc: e.matmul(
                                pb[2][:, :], hT[:, ffl * 2048 + m * 128: ffl * 2048 + (m + 1) * 128],
                                wcT[:, ffl * 1024 + dc * 512: ffl * 1024 + (dc + 1) * 512],
                                start=(ffl == 0), stop=(ffl == 7)))
                        P.wait("dve", "te")
                        P.op("dve", lambda e, m=m, dc=dc: e.tensor_add(
                            res[:, m * 1024 + dc * 512: m * 1024 + (dc + 1) * 512],
                            res[:, m * 1024 + dc * 512: m * 1024 + (dc + 1) * 512], pb[2][:, :]))
                P.flush()
    if stage == 3:
        P.wait("pool", "sp")
        for m in range(16):
            P.dma("pool", part_d[m * 128:(m + 1) * 128, :], res[:, m * 1024:(m + 1) * 1024])
        return B.close()

    g_xnt1 = nc.sbuf_tensor("xnt1s", [128, 8 * 2048], BF16)
    xnt1 = g_xnt1.__enter__()
    with nc.sbuf_tensor("ln_s2", [128, 2048], F32) as xs, \
         nc.sbuf_tensor("ln_m2", [128, 4], F32) as mcol:
        for m in range(16):
            P.wait("pool", "dve", "te", "act")
            P.dma("pool", x1_d[m * 128:(m + 1) * 128, :], res[:, m * 1024:(m + 1) * 1024])
            ln_tile(B, P, res[:, m * 1024:(m + 1) * 1024], xs, mcol, paux,
                    lambda dt, m=m: xnt1[:, dt * 2048 + m * 128: dt * 2048 + (m + 1) * 128])
        for dt in range(8):
            P.wait("pool", "dve")
            P.dma("pool", xnt1_d[dt * 128:(dt + 1) * 128, :], xnt1[:, dt * 2048:(dt + 1) * 2048])
        P.flush()
    g_xnt.__exit__(None, None, None)
    g_res.__exit__(None, None, None)

    with nc.sbuf_tensor("wpb", [128, 8 * 2048], BF16) as wpb, \
         nc.sbuf_tensor("wpstg", [128, 2048], F32) as wpstg, \
         nc.sbuf_tensor("bprow", [1, 2048], F32) as bprow, \
         nc.sbuf_tensor("bprowb", [1, 2048], BF16) as bprowb, \
         nc.sbuf_tensor("htile", [128, 2048], BF16) as htile, \
         nc.sbuf_tensor("wrb", [128, 8 * 9], BF16) as wrb, \
         nc.sbuf_tensor("wrf", [128, 1024], F32) as wrf, \
         nc.sbuf_tensor("lgt", [128, 8], F32) as lgt:
        P.dma("sp", bprow[0:1, :], bp_d.ap().rearrange("(o d) -> o d", o=1))
        P.dma("sp", wrf[0:8, :], Wr_d[:, :])
        P.wait("dve", "sp")
        P.op("dve", lambda e: e.tensor_copy(bprowb[0:1, :], bprow[0:1, :]))
        for dt in range(8):
            P.wait("sp", "dve")
            P.dma("sp", wpstg[:, :], Wp_d[dt * 128:(dt + 1) * 128, :])
            P.wait("dve", "sp")
            P.op("dve", lambda e, dt=dt: e.tensor_copy(wpb[:, dt * 2048:(dt + 1) * 2048], wpstg[:, :]))
        with nc.sbuf_tensor("wrbf", [128, 1024], BF16) as wrbf:
            P.wait("dve", "sp")
            P.op("dve", lambda e: e.tensor_copy(wrbf[0:8, :], wrf[0:8, :]))
            P.selfsync("dve")
            for dt in range(8):
                P.wait("te", "dve", "act")
                P.op("te", lambda e, dt=dt: e.transpose(ptb[:, 0:8], wrbf[0:8, dt * 128:(dt + 1) * 128], B.idb[0:8, 0:8]))
                P.wait("dve", "te")
                P.op("dve", lambda e, dt=dt: e.tensor_copy(wrb[:, dt * 9:dt * 9 + 8], ptb[:, 0:8]))
            P.selfsync("dve")
            for m in range(16):
                for dc in range(4):
                    P.wait("te", "dve", "act")
                    for dt in range(8):
                        P.op("te", lambda e, dt=dt, m=m, dc=dc: e.matmul(
                            pb[dc][:, :], xnt1[:, dt * 2048 + m * 128: dt * 2048 + (m + 1) * 128],
                            wpb[:, dt * 2048 + dc * 512: dt * 2048 + (dc + 1) * 512],
                            start=(dt == 0), stop=False))
                    P.op("te", lambda e, dc=dc: e.matmul(
                        pb[dc][:, :], B.onesb[0:1, 0:128], bprowb[0:1, dc * 512:(dc + 1) * 512],
                        start=False, stop=True))
                P.wait("act", "te")
                for dc in range(4):
                    P.op("act", lambda e, dc=dc: e.activation(htile[:, dc * 512:(dc + 1) * 512],
                                                              pb[dc][:, :], AF.Copy))
                P.wait("pool", "act")
                P.dma("pool", h_d[m * 128:(m + 1) * 128, :], htile[:, :])
                P.wait("te", "act", "dve")
                for dt in range(8):
                    P.op("te", lambda e, dt=dt, m=m: e.matmul(
                        paux[:, 0:8], xnt1[:, dt * 2048 + m * 128: dt * 2048 + (m + 1) * 128],
                        wrb[:, dt * 9:dt * 9 + 8], start=(dt == 0), stop=(dt == 7)))
                P.wait("dve", "te")
                P.op("dve", lambda e: e.tensor_copy(lgt[:, :], paux[:, 0:8]))
                P.wait("pool", "dve")
                P.dma("pool", logits_d[m * 128:(m + 1) * 128, :], lgt[:, :])
            P.flush()
    g_xnt1.__exit__(None, None, None)
    if stage == 4:
        with nc.sbuf_tensor("dbgh", [128, 2048], BF16) as dbgh, \
             nc.sbuf_tensor("dbgf", [128, 1024], F32) as dbgf:
            for m in range(16):
                P.wait("sp", "dve", "te", "act", "pool")
                P.dma("sp", dbgh[:, :], h_d[m * 128:(m + 1) * 128, :])
                P.wait("dve", "sp")
                P.op("dve", lambda e: e.tensor_copy(dbgf[:, :], dbgh[:, 0:1024]))
                P.selfsync("dve")
                P.wait("pool", "dve")
                P.dma("pool", part_d[m * 128:(m + 1) * 128, :], dbgf[:, :])
            P.flush()
        return B.close()

    S = B.sb("S", [128, 24 * 1024], BF16)
    state = B.sb("state", [128, 24], F32)
    with nc.sbuf_tensor("gcols", [128, 180], F32) as gcols, \
         nc.sbuf_tensor("g2b", [128, 2048], F32) as g2b, \
         nc.sbuf_tensor("rowt", [1, 2048], F32) as rowt:
        def layer(i):
            P.wait("sp", "dve", "te", "act", "pool")
            for c in range(16):
                P.dma("sp", gcols[:, c:c + 1], g1_d[i].rearrange("(p o) -> p o", o=1)[c * 128:(c + 1) * 128, :])
                P.dma("sp", gcols[:, 88 + c:89 + c], bo1_d[i].rearrange("(p o) -> p o", o=1)[c * 128:(c + 1) * 128, :])
            for c in range(48):
                P.dma("sp", gcols[:, 120 + c:121 + c], bhg_d[i].rearrange("(p o) -> p o", o=1)[c * 128:(c + 1) * 128, :])
            P.dma("sp", rowt[0:1, :], g2_d[i].rearrange("(o d) -> o d", o=1))
            P.wait("dve", "sp")
            P.op("dve", lambda e: e.tensor_scalar(gcols[:, 0:16], gcols[:, 0:16], 1.0, None, ALU.add))
            P.op("dve", lambda e: e.tensor_scalar(gcols[:, 16:40], gcols[:, 120:144], 0.5, None, ALU.add))
            P.op("dve", lambda e: e.tensor_scalar(gcols[:, 40:64], gcols[:, 144:168], -1.0, None, ALU.mult))
            P.op("dve", lambda e: e.tensor_copy(gcols[:, 64:88], gcols[:, 144:168]))
            P.op("dve", lambda e: e.tensor_scalar(rowt[0:1, :], rowt[0:1, :], 1.0, None, ALU.add))
            P.selfsync("dve")
            for dc in range(4):
                P.wait("te", "dve")
                P.op("te", lambda e, dc=dc: e.matmul(pb[0][:, :], B.onesf[0:1, 0:128],
                                                     rowt[0:1, dc * 512:(dc + 1) * 512], start=True, stop=True))
                P.wait("dve", "te")
                P.op("dve", lambda e, dc=dc: e.tensor_copy(g2b[:, dc * 512:(dc + 1) * 512], pb[0][:, :]))
            P.selfsync("dve")

            for th in range(2):
                toff = th * 1024
                sfx = f"_{i}_{th}"
                with nc.sbuf_tensor("xnT" + sfx, [128, 16 * 1024], BF16) as xnT, \
                     nc.sbuf_tensor("hrt" + sfx, [128, 2048], BF16) as hrt, \
                     nc.sbuf_tensor("rs1" + sfx, [128, 2048], F32) as rs1, \
                     nc.sbuf_tensor("rm1" + sfx, [128, 4], F32) as rm1, \
                     nc.sbuf_tensor("xnf" + sfx, [128, 2048], F32) as xnf:
                    for ml in range(8):
                        m = th * 8 + ml
                        P.wait("sp", "dve", "te", "act")
                        P.dma("sp", hrt[:, :], h_d[m * 128:(m + 1) * 128, :])
                        P.wait("dve", "sp")
                        P.op("dve", lambda e: e.tensor_mul(rs1[:, :], hrt[:, :], hrt[:, :]))
                        P.selfsync("dve")
                        P.op("dve", lambda e: e.reduce_sum(rm1[:, 0:1], rs1[:, :], axis=AX.X))
                        P.wait("act", "dve")
                        P.op("act", lambda e: e.sqrt(rm1[:, 1:2], rm1[:, 0:1]))
                        P.wait("dve", "act")
                        P.op("dve", lambda e: e.tensor_single_scalar(rm1[:, 2:3], rm1[:, 1:2], 1e-12, ALU.max))
                        P.selfsync("dve")
                        P.op("dve", lambda e: e.reciprocal(rm1[:, 3:4], rm1[:, 2:3]))
                        P.selfsync("dve")
                        P.op("dve", lambda e: e.tensor_scalar(rm1[:, 3:4], rm1[:, 3:4], SQH, None, ALU.mult))
                        P.selfsync("dve")
                        P.op("dve", lambda e: e.tensor_scalar(xnf[:, :], hrt[:, :], rm1[:, 3:4], None, ALU.mult))
                        P.selfsync("dve")
                        for c in range(16):
                            tr_f32(B, P, xnf[:, c * 128:(c + 1) * 128],
                                   xnT[:, c * 1024 + ml * 128: c * 1024 + (ml + 1) * 128],
                                   scale_col=gcols[:, c:c + 1], paux=paux)
                    with nc.sbuf_tensor("whgf" + sfx, [128, 2048], F32) as whgf, \
                         nc.sbuf_tensor("whgb" + sfx, [128, 2 * 2048], BF16) as whgb, \
                         nc.sbuf_tensor("ab" + sfx, [128, 4 * 1024], F32) as ab, \
                         nc.sbuf_tensor("scf" + sfx, [128, 1024], F32) as scf:
                        for j in range(24):
                            for gi, fj in ((0, j), (1, j + 24)):
                                P.wait("sp", "dve", "te")
                                dma_kload(P, "sp", whgf[:, :], Whg_d[i][:, fj * 128:(fj + 1) * 128], 16, nsplit=2)
                                P.wait("dve", "sp")
                                P.op("dve", lambda e, gi=gi: e.tensor_copy(whgb[:, gi * 2048:(gi + 1) * 2048], whgf[:, :]))
                                P.selfsync("dve")
                                for tc in range(2):
                                    bank = pb[2 * gi + tc]
                                    P.wait("te", "act", "dve")
                                    for k in range(16):
                                        P.op("te", lambda e, b=bank, gi=gi, k=k, tc=tc, toff=toff: e.matmul(
                                            b[:, :], whgb[:, gi * 2048 + k * 128: gi * 2048 + (k + 1) * 128],
                                            xnT[:, k * 1024 + tc * 512: k * 1024 + (tc + 1) * 512],
                                            start=(k == 0), stop=(k == 15)))
                            P.wait("act", "te")
                            for tc in range(2):
                                sl = slice(tc * 512, (tc + 1) * 512)
                                P.op("act", lambda e, tc=tc, j=j: e.activation(
                                    ab[:, tc * 512:(tc + 1) * 512], pb[2 + tc][:, :], AF.Sigmoid,
                                    bias=gcols[:, 40 + j:41 + j], scale=-1.0))
                                P.op("act", lambda e, tc=tc, j=j: e.activation(
                                    ab[:, 1024 + tc * 512:1024 + (tc + 1) * 512], pb[2 + tc][:, :], AF.Sigmoid,
                                    bias=gcols[:, 64 + j:65 + j], scale=1.0))
                                P.op("act", lambda e, tc=tc, j=j: e.activation(
                                    ab[:, 2048 + tc * 512:2048 + (tc + 1) * 512], pb[tc][:, :], AF.Identity,
                                    bias=gcols[:, 16 + j:17 + j], scale=1.0))
                                P.op("act", lambda e, tc=tc, j=j: e.activation(
                                    ab[:, 3072 + tc * 512:3072 + (tc + 1) * 512], pb[tc][:, :], AF.Sigmoid,
                                    bias=gcols[:, 120 + j:121 + j], scale=1.0))
                            P.wait("dve", "act")
                            P.op("dve", lambda e: e.tensor_single_scalar(scf[:, :], ab[:, 2048:3072], 0.5, ALU.is_ge))
                            P.selfsync("dve")
                            P.op("dve", lambda e: e.tensor_sub(ab[:, 2048:3072], ab[:, 2048:3072], ab[:, 3072:4096]))
                            P.selfsync("dve")
                            P.op("dve", lambda e: e.tensor_mul(ab[:, 2048:3072], ab[:, 2048:3072], scf[:, :]))
                            P.selfsync("dve")
                            P.op("dve", lambda e: e.tensor_add(ab[:, 2048:3072], ab[:, 2048:3072], ab[:, 3072:4096]))
                            P.selfsync("dve")
                            P.op("dve", lambda e: e.tensor_mul(ab[:, 1024:2048], ab[:, 1024:2048], ab[:, 2048:3072]))
                            P.selfsync("dve")
                            if th == 0:
                                P.op("dve", lambda e: e.tensor_tensor_scan(
                                    scf[:, :], ab[:, 0:1024], ab[:, 1024:2048], 1.0, ALU.mult, ALU.add))
                            else:
                                P.op("dve", lambda e, j=j: e.tensor_tensor_scan(
                                    scf[:, :], ab[:, 0:1024], ab[:, 1024:2048], state[:, j:j + 1],
                                    ALU.mult, ALU.add))
                            P.selfsync("dve")
                            P.op("dve", lambda e, j=j: e.tensor_copy(S[:, j * 1024:(j + 1) * 1024], scf[:, :]))
                            if th == 0:
                                P.op("dve", lambda e, j=j: e.tensor_copy(state[:, j:j + 1], scf[:, 1023:1024]))
                            P.selfsync("dve")
                    P.flush()
                with nc.sbuf_tensor("hp" + sfx, [128, 8 * 2048], BF16) as hp, \
                     nc.sbuf_tensor("wof" + sfx, [128, 12 * 128], F32) as wof, \
                     nc.sbuf_tensor("wob1" + sfx, [128, 24 * 128], BF16) as wob1, \
                     nc.sbuf_tensor("gfc" + sfx, [128, 1024], BF16) as gfc, \
                     nc.sbuf_tensor("r2s" + sfx, [128, 2048], F32) as r2s, \
                     nc.sbuf_tensor("r2m" + sfx, [128, 4], F32) as r2m:
                    for ml in range(8):
                        m = th * 8 + ml
                        P.wait("sp", "dve", "te", "act")
                        P.dma("sp", hp[:, ml * 2048:(ml + 1) * 2048], h_d[m * 128:(m + 1) * 128, :])
                    for fc in range(16):
                        P.wait("sp", "dve", "te")
                        src3 = Wo1_d[i][:, fc * 128:(fc + 1) * 128].rearrange("(k p) f -> p k f", p=128)
                        for half in range(2):
                            P.dma("sp", wof[:, :].rearrange("p (k f) -> p k f", k=12),
                                  src3[:, half * 12:(half + 1) * 12])
                            P.wait("dve", "sp")
                            P.op("dve", lambda e, half=half: e.tensor_copy(
                                wob1[:, half * 1536:(half + 1) * 1536], wof[:, :]))
                            P.selfsync("dve")
                            P.wait("sp", "dve")
                        for tc in range(2):
                            P.wait("te", "act", "dve")
                            for j in range(24):
                                P.op("te", lambda e, j=j, tc=tc, toff=toff: e.matmul(
                                    pb[4][:, :], wob1[:, j * 128:(j + 1) * 128],
                                    S[:, j * 1024 + tc * 512: j * 1024 + (tc + 1) * 512],
                                    start=(j == 0), stop=(j == 23)))
                            P.wait("act", "te")
                            P.op("act", lambda e, fc=fc, tc=tc: e.activation(
                                gfc[:, tc * 512:(tc + 1) * 512], pb[4][:, :], AF.Identity,
                                bias=gcols[:, 88 + fc:89 + fc], scale=1.0))
                        for ml in range(8):
                            P.wait("te", "act", "dve")
                            P.op("te", lambda e, ml=ml: e.transpose(
                                ptb[:, 0:128], gfc[:, ml * 128:(ml + 1) * 128], B.idb[:, :]))
                            P.wait("dve", "te")
                            P.op("dve", lambda e, ml=ml, fc=fc: e.tensor_add(
                                hp[:, ml * 2048 + fc * 128: ml * 2048 + (fc + 1) * 128],
                                hp[:, ml * 2048 + fc * 128: ml * 2048 + (fc + 1) * 128],
                                ptb[:, 0:128]))
                    for ml in range(8):
                        m = th * 8 + ml
                        P.wait("dve", "te", "act", "pool")
                        P.op("dve", lambda e, ml=ml: e.tensor_mul(r2s[:, :], hp[:, ml * 2048:(ml + 1) * 2048],
                                                                  hp[:, ml * 2048:(ml + 1) * 2048]))
                        P.selfsync("dve")
                        P.op("dve", lambda e: e.reduce_sum(r2m[:, 0:1], r2s[:, :], axis=AX.X))
                        P.wait("act", "dve")
                        P.op("act", lambda e: e.sqrt(r2m[:, 1:2], r2m[:, 0:1]))
                        P.wait("dve", "act")
                        P.op("dve", lambda e: e.tensor_single_scalar(r2m[:, 2:3], r2m[:, 1:2], 1e-12, ALU.max))
                        P.selfsync("dve")
                        P.op("dve", lambda e: e.reciprocal(r2m[:, 3:4], r2m[:, 2:3]))
                        P.selfsync("dve")
                        P.op("dve", lambda e: e.tensor_scalar(r2m[:, 3:4], r2m[:, 3:4], SQH, None, ALU.mult))
                        P.selfsync("dve")
                        P.op("dve", lambda e, ml=ml: e.tensor_scalar(r2s[:, :], hp[:, ml * 2048:(ml + 1) * 2048],
                                                                     r2m[:, 3:4], None, ALU.mult))
                        P.selfsync("dve")
                        P.op("dve", lambda e: e.tensor_mul(r2s[:, :], r2s[:, :], g2b[:, :]))
                        P.selfsync("dve")
                        P.op("dve", lambda e, ml=ml: e.tensor_add(hp[:, ml * 2048:(ml + 1) * 2048],
                                                                  hp[:, ml * 2048:(ml + 1) * 2048], r2s[:, :]))
                        P.selfsync("dve")
                        P.wait("pool", "dve")
                        P.dma("pool", h_d[m * 128:(m + 1) * 128, :], hp[:, ml * 2048:(ml + 1) * 2048])
                    P.flush()
        for i in range(4):
            layer(i)

        P.wait("sp", "dve", "te", "act", "pool")
        for c in range(16):
            P.dma("sp", gcols[:, 104 + c:105 + c], gf_d.ap().rearrange("(p o) -> p o", o=1)[c * 128:(c + 1) * 128, :])
        P.wait("dve", "sp")
        P.op("dve", lambda e: e.tensor_scalar(gcols[:, 104:120], gcols[:, 104:120], 1.0, None, ALU.add))
        P.selfsync("dve")
        with nc.sbuf_tensor("wfb", [128, 16 * 1024], BF16) as wfb, \
             nc.sbuf_tensor("wfstg", [128, 1024], F32) as wfstg, \
             nc.sbuf_tensor("bfrow", [1, 1024], F32) as bfrow, \
             nc.sbuf_tensor("bfrowb", [1, 1024], BF16) as bfrowb, \
             nc.sbuf_tensor("x1t", [128, 1024], F32) as x1t, \
             nc.sbuf_tensor("prt", [128, 1024], F32) as prt:
            P.dma("sp", bfrow[0:1, :], bf_d.ap().rearrange("(o d) -> o d", o=1))
            for c in range(16):
                P.wait("sp", "dve")
                P.dma("sp", wfstg[:, :], Wf_d[c * 128:(c + 1) * 128, :])
                P.wait("dve", "sp")
                P.op("dve", lambda e, c=c: e.tensor_copy(wfb[:, c * 1024:(c + 1) * 1024], wfstg[:, :]))
            P.op("dve", lambda e: e.tensor_copy(bfrowb[0:1, :], bfrow[0:1, :]))
            P.selfsync("dve")
            for th in range(2):
                with nc.sbuf_tensor(f"xfT_{th}", [128, 16 * 1024], BF16) as xfT, \
                     nc.sbuf_tensor(f"hrt2_{th}", [128, 2048], BF16) as hrt2, \
                     nc.sbuf_tensor(f"rs3_{th}", [128, 2048], F32) as rs3, \
                     nc.sbuf_tensor(f"rm3_{th}", [128, 4], F32) as rm3, \
                     nc.sbuf_tensor(f"xnf3_{th}", [128, 2048], F32) as xnf3:
                    for ml in range(8):
                        m = th * 8 + ml
                        P.wait("sp", "dve", "te", "act")
                        P.dma("sp", hrt2[:, :], h_d[m * 128:(m + 1) * 128, :])
                        P.wait("dve", "sp")
                        P.op("dve", lambda e: e.tensor_mul(rs3[:, :], hrt2[:, :], hrt2[:, :]))
                        P.selfsync("dve")
                        P.op("dve", lambda e: e.reduce_sum(rm3[:, 0:1], rs3[:, :], axis=AX.X))
                        P.wait("act", "dve")
                        P.op("act", lambda e: e.sqrt(rm3[:, 1:2], rm3[:, 0:1]))
                        P.wait("dve", "act")
                        P.op("dve", lambda e: e.tensor_single_scalar(rm3[:, 2:3], rm3[:, 1:2], 1e-12, ALU.max))
                        P.selfsync("dve")
                        P.op("dve", lambda e: e.reciprocal(rm3[:, 3:4], rm3[:, 2:3]))
                        P.selfsync("dve")
                        P.op("dve", lambda e: e.tensor_scalar(rm3[:, 3:4], rm3[:, 3:4], SQH, None, ALU.mult))
                        P.selfsync("dve")
                        P.op("dve", lambda e: e.tensor_scalar(xnf3[:, :], hrt2[:, :], rm3[:, 3:4], None, ALU.mult))
                        P.selfsync("dve")
                        for c in range(16):
                            tr_f32(B, P, xnf3[:, c * 128:(c + 1) * 128],
                                   xfT[:, c * 1024 + ml * 128: c * 1024 + (ml + 1) * 128],
                                   scale_col=gcols[:, 104 + c:105 + c], paux=paux)
                    for ml in range(8):
                        m = th * 8 + ml
                        P.wait("sp", "te", "dve", "act")
                        P.dma("sp", x1t[:, :], x1_d[m * 128:(m + 1) * 128, :])
                        for dc in range(2):
                            P.wait("te", "dve", "act")
                            for c in range(16):
                                P.op("te", lambda e, c=c, ml=ml, dc=dc: e.matmul(
                                    pb[0][:, :], xfT[:, c * 1024 + ml * 128: c * 1024 + (ml + 1) * 128],
                                    wfb[:, c * 1024 + dc * 512: c * 1024 + (dc + 1) * 512],
                                    start=(c == 0), stop=False))
                            P.op("te", lambda e, dc=dc: e.matmul(
                                pb[0][:, :], B.onesb[0:1, 0:128], bfrowb[0:1, dc * 512:(dc + 1) * 512],
                                start=False, stop=True))
                            P.wait("dve", "te", "sp")
                            P.op("dve", lambda e, dc=dc: e.tensor_add(
                                prt[:, dc * 512:(dc + 1) * 512], x1t[:, dc * 512:(dc + 1) * 512], pb[0][:, :]))
                        P.selfsync("dve")
                        P.wait("pool", "dve")
                        P.dma("pool", part_d[m * 128:(m + 1) * 128, :], prt[:, :])
                    P.flush()
    return B.close()


def build_C():
    B = Builder("C")
    nc = B.nc
    lg_d = nc.dram_tensor("lg", [8, 128, 128], F32, kind="ExternalInput")
    wpp_d = nc.dram_tensor("wpp", [8, 128, 128], F32, kind="ExternalOutput")
    fb_d = nc.dram_tensor("fbo", [128, 128], F32, kind="ExternalOutput")
    paux = nc.alloc_psum_tensor("paux", [128, 512], F32)

    P = B.start()
    consts(B, P)
    U = nc.alloc_sbuf_tensor("U", [128, 128], F32)
    P.wait("pool", "dve")
    P.op("pool", lambda e: e.affine_select(U[:, :], B.onesf[:, :], [[1, 128]],
                                           ALU.is_gt, 0.0, base=0, channel_multiplier=-1))
    pr = B.sb("pr", [128, 8 * 128], F32)
    kept = B.sb("kept", [128, 8 * 128], F32)
    t0 = B.sb("t0", [128, 128], F32)
    t1 = B.sb("t1", [128, 128], F32)
    m1 = B.sb("m1", [128, 128], F32)
    m2 = B.sb("m2", [128, 128], F32)
    cnt = B.sb("cnt", [128, 128], F32)
    fb = B.sb("fb", [128, 128], F32)
    colsC = B.sb("colsC", [128, 4], F32)

    def E(j):
        return pr[:, j * 128:(j + 1) * 128]

    for e8 in range(8):
        P.dma("sp", E(e8), lg_d[e8])
    P.wait("dve", "sp")
    P.op("dve", lambda e: e.tensor_copy(m1[:, :], E(0)))
    P.selfsync("dve")
    for e8 in range(1, 8):
        P.op("dve", lambda e, e8=e8: e.tensor_max(m1[:, :], m1[:, :], E(e8)))
        P.selfsync("dve")
    for e8 in range(8):
        P.op("dve", lambda e, e8=e8: e.tensor_sub(E(e8), E(e8), m1[:, :]))
    P.selfsync("dve")
    P.wait("act", "dve")
    for e8 in range(8):
        P.op("act", lambda e, e8=e8: e.activation(E(e8), E(e8), AF.Exp))
    P.wait("dve", "act")
    P.op("dve", lambda e: e.tensor_copy(t0[:, :], E(0)))
    P.selfsync("dve")
    for e8 in range(1, 8):
        P.op("dve", lambda e, e8=e8: e.tensor_add(t0[:, :], t0[:, :], E(e8)))
        P.selfsync("dve")
    P.op("dve", lambda e: e.reciprocal(t1[:, :], t0[:, :]))
    P.selfsync("dve")
    for e8 in range(8):
        P.op("dve", lambda e, e8=e8: e.tensor_mul(E(e8), E(e8), t1[:, :]))
    P.selfsync("dve")
    P.op("dve", lambda e: e.tensor_copy(m1[:, :], E(0)))
    P.selfsync("dve")
    for e8 in range(1, 8):
        P.op("dve", lambda e, e8=e8: e.tensor_max(m1[:, :], m1[:, :], E(e8)))
        P.selfsync("dve")
    P.op("dve", lambda e: e.memset(cnt[:, :], 0.0))
    P.op("dve", lambda e: e.memset(m2[:, :], -1.0))
    P.selfsync("dve")
    for e8 in range(8):
        P.op("dve", lambda e, e8=e8: e.tensor_tensor(t0[:, :], E(e8), m1[:, :], ALU.is_equal))
        P.selfsync("dve")
        P.op("dve", lambda e: e.tensor_add(cnt[:, :], cnt[:, :], t0[:, :]))
        P.selfsync("dve")
        P.op("dve", lambda e, e8=e8: e.tensor_tensor(t0[:, :], E(e8), m1[:, :], ALU.is_lt))
        P.selfsync("dve")
        P.op("dve", lambda e, e8=e8: e.tensor_mul(t1[:, :], E(e8), t0[:, :]))
        P.selfsync("dve")
        P.op("dve", lambda e: e.tensor_scalar(t0[:, :], t0[:, :], -1.0, None, ALU.add))
        P.selfsync("dve")
        P.op("dve", lambda e: e.tensor_add(t1[:, :], t1[:, :], t0[:, :]))
        P.selfsync("dve")
        P.op("dve", lambda e: e.tensor_max(m2[:, :], m2[:, :], t1[:, :]))
        P.selfsync("dve")
    P.op("dve", lambda e: e.tensor_single_scalar(t0[:, :], cnt[:, :], 2.0, ALU.is_ge))
    P.selfsync("dve")
    P.op("dve", lambda e: e.tensor_sub(t1[:, :], m1[:, :], m2[:, :]))
    P.selfsync("dve")
    P.op("dve", lambda e: e.tensor_mul(t1[:, :], t1[:, :], t0[:, :]))
    P.selfsync("dve")
    P.op("dve", lambda e: e.tensor_add(m2[:, :], m2[:, :], t1[:, :]))
    P.selfsync("dve")
    P.op("dve", lambda e: e.tensor_add(t0[:, :], m1[:, :], m2[:, :]))
    P.selfsync("dve")
    P.op("dve", lambda e: e.tensor_single_scalar(t0[:, :], t0[:, :], 1e-9, ALU.max))
    P.selfsync("dve")
    P.op("dve", lambda e: e.reciprocal(t1[:, :], t0[:, :]))
    P.selfsync("dve")
    P.op("dve", lambda e: e.memset(fb[:, :], 0.0))
    P.selfsync("dve")
    for e8 in range(8):
        P.op("dve", lambda e, e8=e8: e.tensor_tensor(t0[:, :], E(e8), m2[:, :], ALU.is_ge))
        P.selfsync("dve")
        P.op("dve", lambda e, e8=e8: e.tensor_mul(E(e8), E(e8), t1[:, :]))
        P.selfsync("dve")
        P.op("dve", lambda e: e.tensor_scalar(cnt[:, :], fb[:, :], -1.0, 1.0, ALU.mult, ALU.add))
        P.selfsync("dve")
        P.op("dve", lambda e: e.tensor_mul(t0[:, :], t0[:, :], cnt[:, :]))
        P.selfsync("dve")
        P.op("dve", lambda e: e.tensor_tensor_scan(m1[:, :], B.onesf[:, :], t0[:, :], 0.0, ALU.mult, ALU.add))
        P.selfsync("dve")
        P.op("dve", lambda e: e.reduce_sum(colsC[:, 0:1], t0[:, :], axis=AX.X))
        P.selfsync("dve")
        P.wait("te", "dve")
        P.op("te", lambda e: e.matmul(paux[:, 0:1], U[:, :], colsC[:, 0:1], start=True, stop=True))
        P.wait("dve", "te")
        P.op("dve", lambda e: e.tensor_copy(colsC[:, 1:2], paux[:, 0:1]))
        P.selfsync("dve")
        P.op("dve", lambda e: e.tensor_scalar(m1[:, :], m1[:, :], colsC[:, 1:2], -1.0, ALU.add, ALU.add))
        P.selfsync("dve")
        P.op("dve", lambda e: e.tensor_single_scalar(m1[:, :], m1[:, :], CAP, ALU.is_lt))
        P.selfsync("dve")
        P.op("dve", lambda e, e8=e8: e.tensor_mul(kept[:, e8 * 128:(e8 + 1) * 128], t0[:, :], m1[:, :]))
        P.selfsync("dve")
        P.op("dve", lambda e, e8=e8: e.tensor_sub(t0[:, :], t0[:, :], kept[:, e8 * 128:(e8 + 1) * 128]))
        P.selfsync("dve")
        P.op("dve", lambda e: e.tensor_add(fb[:, :], fb[:, :], t0[:, :]))
        P.selfsync("dve")
    P.op("dve", lambda e: e.tensor_scalar(cnt[:, :], fb[:, :], -1.0, 1.0, ALU.mult, ALU.add))
    P.selfsync("dve")
    for e8 in range(8):
        P.op("dve", lambda e, e8=e8: e.tensor_mul(E(e8), E(e8), kept[:, e8 * 128:(e8 + 1) * 128]))
        P.selfsync("dve")
        P.op("dve", lambda e, e8=e8: e.tensor_mul(E(e8), E(e8), cnt[:, :]))
        P.selfsync("dve")
        P.wait("pool", "dve")
        P.dma("pool", wpp_d[e8], E(e8))
    P.dma("pool", fb_d[:, :], fb[:, :])
    return B.close()


def build_D():
    B = Builder("D")
    nc = B.nc
    part_d = nc.dram_tensor("part", [TOK, D], F32, kind="ExternalInput")
    xnt1_d = nc.dram_tensor("xnt1", [D, TOK], BF16, kind="ExternalInput")
    wloc_d = nc.dram_tensor("wloc", [9, 16, 128], F32, kind="ExternalInput")
    W1_d = nc.dram_tensor("W1", [8, D, 4096], F32, kind="ExternalInput")
    b1_d = nc.dram_tensor("b1", [8, 4096], F32, kind="ExternalInput")
    W2_d = nc.dram_tensor("W2", [8, 4096, D], F32, kind="ExternalInput")
    b2_d = nc.dram_tensor("b2", [8, D], F32, kind="ExternalInput")
    fw1_d = nc.dram_tensor("fw1", [D, 4096], F32, kind="ExternalInput")
    fb1_d = nc.dram_tensor("fb1", [4096], F32, kind="ExternalInput")
    fw2_d = nc.dram_tensor("fw2", [4096, D], F32, kind="ExternalInput")
    fb2_d = nc.dram_tensor("fb2", [D], F32, kind="ExternalInput")
    out_d = nc.dram_tensor("out", [TOK, D], F32, kind="ExternalOutput")

    pb = [nc.alloc_psum_tensor(f"pb{i}", [128, 512], F32) for i in range(6)]

    P = B.start()
    consts(B, P)
    xnt1 = B.sb("xnt1s", [128, 8 * 2048], BF16)
    yacc = B.sb("yacc", [128, 8 * 1024], F32)
    wcols = B.sb("wcols", [128, 9 * 16], F32)
    b1c = B.sb("b1c", [128, 32], F32)
    b2row = B.sb("b2row", [1, 1024], F32)
    b2rowb = B.sb("b2rowb", [1, 1024], BF16)

    for dt in range(8):
        P.dma("sp", xnt1[:, dt * 2048:(dt + 1) * 2048], xnt1_d[dt * 128:(dt + 1) * 128, :])
    for e9 in range(9):
        for j in range(16):
            P.dma("sp", wcols[:, e9 * 16 + j:e9 * 16 + j + 1],
                  wloc_d[e9, j].rearrange("(p o) -> p o", o=1))

    with nc.sbuf_tensor("HT", [128, 32 * 1024], BF16) as HT, \
         nc.sbuf_tensor("w1f", [128, 1024], F32) as w1f, \
         nc.sbuf_tensor("w1b", [128, 1024], BF16) as w1b, \
         nc.sbuf_tensor("w2f", [128, 8 * 512], F32) as w2f, \
         nc.sbuf_tensor("w2b", [128, 32 * 512], BF16) as w2b, \
         nc.sbuf_tensor("ptile", [128, 1024], F32) as ptile:
        for th in range(2):
            toff = th * 1024
            P.wait("dve", "te", "act", "sp", "pool")
            P.op("dve", lambda e: e.memset(yacc[:, :], 0.0))
            for e9 in range(9):
                for ft in range(32):
                    P.wait("sp", "dve", "te")
                    if e9 < 8:
                        src = W1_d[e9][:, ft * 128:(ft + 1) * 128]
                        bsrc = b1_d[e9].rearrange("(p o) -> p o", o=1)[ft * 128:(ft + 1) * 128, :]
                    else:
                        src = fw1_d[:, ft * 128:(ft + 1) * 128]
                        bsrc = fb1_d.ap().rearrange("(p o) -> p o", o=1)[ft * 128:(ft + 1) * 128, :]
                    dma_kload(P, "sp", w1f[:, :], src, 8)
                    P.dma("sp", b1c[:, ft % 32:ft % 32 + 1], bsrc)
                    P.wait("dve", "sp")
                    P.op("dve", lambda e: e.tensor_copy(w1b[:, :], w1f[:, :]))
                    P.selfsync("dve")
                    for tc in range(2):
                        P.wait("te", "act", "dve")
                        for dt in range(8):
                            P.op("te", lambda e, dt=dt, tc=tc, toff=toff: e.matmul(
                                pb[tc][:, :], w1b[:, dt * 128:(dt + 1) * 128],
                                xnt1[:, dt * 2048 + toff + tc * 512: dt * 2048 + toff + (tc + 1) * 512],
                                start=(dt == 0), stop=(dt == 7)))
                        P.wait("act", "te")
                        P.op("act", lambda e, ft=ft, tc=tc: e.activation(
                            HT[:, ft * 1024 + tc * 512: ft * 1024 + (tc + 1) * 512],
                            pb[tc][:, :], AF.Gelu, bias=b1c[:, ft % 32:ft % 32 + 1], scale=1.0))
                P.wait("sp", "dve", "te", "act")
                if e9 < 8:
                    P.dma("sp", b2row[0:1, :], b2_d[e9].rearrange("(o d) -> o d", o=1))
                else:
                    P.dma("sp", b2row[0:1, :], fb2_d.ap().rearrange("(o d) -> o d", o=1))
                P.wait("dve", "sp")
                P.op("dve", lambda e: e.tensor_copy(b2rowb[0:1, :], b2row[0:1, :]))
                P.selfsync("dve")
                for dc in range(2):
                    for kb in range(4):
                        P.wait("sp", "dve")
                        if e9 < 8:
                            src = W2_d[e9][kb * 1024:(kb + 1) * 1024, dc * 512:(dc + 1) * 512]
                        else:
                            src = fw2_d[kb * 1024:(kb + 1) * 1024, dc * 512:(dc + 1) * 512]
                        dma_kload(P, "sp", w2f[:, :], src, 8)
                        P.wait("dve", "sp")
                        P.op("dve", lambda e, kb=kb: e.tensor_copy(
                            w2b[:, kb * 8 * 512:(kb + 1) * 8 * 512], w2f[:, :]))
                        P.selfsync("dve")
                    for ml in range(8):
                        P.wait("te", "act", "dve")
                        for ft in range(32):
                            P.op("te", lambda e, ft=ft, ml=ml, dc=dc: e.matmul(
                                pb[4][:, :], HT[:, ft * 1024 + ml * 128: ft * 1024 + (ml + 1) * 128],
                                w2b[:, ft * 512:(ft + 1) * 512],
                                start=(ft == 0), stop=False))
                        P.op("te", lambda e, dc=dc: e.matmul(
                            pb[4][:, :], B.onesb[0:1, 0:128], b2rowb[0:1, dc * 512:(dc + 1) * 512],
                            start=False, stop=True))
                        P.wait("dve", "te")
                        j = th * 8 + ml
                        P.op("dve", lambda e, ml=ml, dc=dc, e9=e9, j=j: e.scalar_tensor_tensor(
                            yacc[:, ml * 1024 + dc * 512: ml * 1024 + (dc + 1) * 512],
                            pb[4][:, :], wcols[:, e9 * 16 + j:e9 * 16 + j + 1],
                            yacc[:, ml * 1024 + dc * 512: ml * 1024 + (dc + 1) * 512],
                            ALU.mult, ALU.add))
                        P.selfsync("dve")
                P.flush()
            for ml in range(8):
                m = th * 8 + ml
                P.wait("sp", "dve", "te", "act")
                P.dma("sp", ptile[:, :], part_d[m * 128:(m + 1) * 128, :])
                P.wait("dve", "sp")
                P.op("dve", lambda e, ml=ml: e.tensor_add(ptile[:, :], ptile[:, :],
                                                          yacc[:, ml * 1024:(ml + 1) * 1024]))
                P.selfsync("dve")
                P.wait("pool", "dve")
                P.dma("pool", out_d[m * 128:(m + 1) * 128, :], ptile[:, :])
            P.flush()
    return B.close()


_CACHE = {}


def kernel(**inp):
    if "A" not in _CACHE:
        _CACHE["A"] = build_A()
        _CACHE["C"] = build_C()
        _CACHE["D"] = build_D()
    ncA, ncC, ncD = _CACHE["A"], _CACHE["C"], _CACHE["D"]
    cores = list(range(8))
    f = np.ascontiguousarray

    common_A = dict(
        Wq=f(inp["L0_attn_Wq"]), bq=f(inp["L0_attn_bq"]), Wk=f(inp["L0_attn_Wk"]),
        bk=f(inp["L0_attn_bk"]), Wv=f(inp["L0_attn_Wv"]), bv=f(inp["L0_attn_bv"]),
        Wo=f(inp["L0_attn_Wo"]), bo=f(inp["L0_attn_bo"]), rm=f(inp["L0_ffn_rail_maker"]),
        rup=f(inp["L0_ffn_rail_up_proj"]), wc=f(inp["L0_ffn_w_cproj"]),
        Wp=f(inp["L1_attn_Wp"]), bp=f(inp["L1_attn_bp"]), g1=f(inp["L1_attn_g1"]),
        Whg=f(inp["L1_attn_Whg"]), bhg=f(inp["L1_attn_bhg"]), Wo1=f(inp["L1_attn_Wo"]),
        bo1=f(inp["L1_attn_bo"]), g2=f(inp["L1_attn_g2"]), gf=f(inp["L1_attn_gf"]),
        Wf=f(inp["L1_attn_Wf"]), bf=f(inp["L1_attn_bf"]), Wr=f(inp["L1_ffn_Wr"]))
    in_A = [dict(common_A, x=f(inp["L0_attn_x"][b])) for b in range(8)]
    rA = run_bass_kernel_spmd(ncA, in_A, core_ids=cores).results

    logits_full = np.concatenate([rA[b]["logits"] for b in range(8)], axis=0)
    lg = np.ascontiguousarray(logits_full.T.reshape(8, 128, 128))
    rC = run_bass_kernel_spmd(ncC, [dict(lg=lg) for _ in range(8)], core_ids=cores).results
    wpp = rC[0]["wpp"].reshape(8, 128, 128)
    fbo = rC[0]["fbo"].reshape(128, 128)

    common_D = dict(
        W1=f(inp["L1_ffn_W1"]), b1=f(inp["L1_ffn_b1"]), W2=f(inp["L1_ffn_W2"]),
        b2=f(inp["L1_ffn_b2"]), fw1=f(inp["L1_ffn_fw1"]), fb1=f(inp["L1_ffn_fb1"]),
        fw2=f(inp["L1_ffn_fw2"]), fb2=f(inp["L1_ffn_fb2"]))
    in_D = []
    for b in range(8):
        wloc = np.empty((9, 16, 128), np.float32)
        for e9 in range(8):
            wloc[e9] = wpp[e9][16 * b:16 * b + 16, :]
        wloc[8] = fbo[16 * b:16 * b + 16, :]
        in_D.append(dict(common_D, part=f(rA[b]["part"]), xnt1=f(rA[b]["xnt1"]),
                         wloc=f(wloc)))
    rD = run_bass_kernel_spmd(ncD, in_D, core_ids=cores).results
    return np.stack([rD[b]["out"] for b in range(8)], axis=0)
